# version 02 (final)
# baseline (speedup 1.0000x reference)
import numpy as np
import ml_dtypes
from contextlib import ExitStack

import concourse.bass as bass
import concourse.mybir as mybir
from concourse.bass_utils import run_bass_kernel_spmd

F32 = mybir.dt.float32
BF16 = mybir.dt.bfloat16
AF = mybir.ActivationFunctionType
ALU = mybir.AluOpType
AX = mybir.AxisListType

D = 1024
T = 2048
NCH = 8
HEADS = 16
HD = 64
CH = 64
NTT = 4
LN_EPS = 1e-5
PAIRS = [[0, 1], [2, 3], [4, 5], [6, 7]]
BF = ml_dtypes.bfloat16
import os
L1F_LIMIT = int(os.environ.get("L1F_LIMIT", "9"))
LN_LIMIT = int(os.environ.get("LN_LIMIT", "9"))
L1A_LIMIT = int(os.environ.get("L1A_LIMIT", "9"))
L1A_SUB = int(os.environ.get("L1A_SUB", "9"))
L0A_LIMIT = int(os.environ.get("L0A_LIMIT", "9"))
L0A_CHUNK = int(os.environ.get("L0A_CHUNK", "9"))


class Ctx:
    def __init__(self, nc, block):
        self.nc = nc
        self.block = block
        self.sems = {}
        self.cnt = {}

    def sem(self, name):
        if name not in self.sems:
            self.sems[name] = self.nc.alloc_semaphore(name)
            self.cnt[name] = 0
        return self.sems[name]

    def inc(self, inst, name, n=1):
        inst.then_inc(self.sem(name), n)
        self.cnt[name] += n
        return self.cnt[name]

    def tok(self, name):
        self.sem(name)
        return self.cnt[name]

    _uid = 0

    def uname(self, base):
        Ctx._uid += 1
        return f"{base}_{Ctx._uid}"

    def incd(self, eng, name, n=1):
        inst = eng.drain()
        inst.then_inc(self.sem(name), n)
        self.cnt[name] += n
        return self.cnt[name]

    def wait_compute(self, e):
        for name in ("pe", "ve", "se", "gp", "gl"):
            if name in self.sems and self.cnt[name] > 0:
                e.wait_ge(self.sems[name], self.cnt[name])


def chunk(buf, c, w=T):
    return buf[:, c * w:(c + 1) * w]


def emit_ln(env, src_bf, mode, zn=None, want_s=False, gamma=None, beta=None):
    nc, block, ctx = env["nc"], env["block"], env["ctx"]
    rowbuf, sq = env["rowbuf"], env["sq"]
    onescol, onesrow = env["onescol"], env["onesrow"]
    x_res = env["x_res"]

    with ExitStack() as ps:
        psS = ps.enter_context(nc.psum_tensor(ctx.uname("psLNS"), [1, T], F32))
        psQ = ps.enter_context(nc.psum_tensor(ctx.uname("psLNQ"), [1, T], F32))

        sq_toks = []
        pe_base = ctx.tok("pe")

        @block.scalar
        def _(s):
            s.wait_ge(ctx.sem("ve"), ctx.tok("ve"))
            if "gl" in ctx.sems:
                s.wait_ge(ctx.sem("gl"), ctx.tok("gl"))
            for c in range(NCH):
                inst = s.activation(sq[:], src_bf[:, c * T:(c + 1) * T], AF.Square)
                sq_toks.append(ctx.inc(inst, "se"))
                if c < NCH - 1:
                    s.wait_ge(ctx.sem("pe"), pe_base + (c + 1))

        @block.tensor
        def _(t):
            t.wait_ge(ctx.sem("ve"), ctx.tok("ve"))
            if "gl" in ctx.sems:
                t.wait_ge(ctx.sem("gl"), ctx.tok("gl"))
            for c in range(NCH):
                t.wait_ge(ctx.sem("se"), sq_toks[c])
                for i in range(NTT):
                    sl = slice(i * 512, (i + 1) * 512)
                    t.matmul(psS[0:1, sl], onescol[:],
                             src_bf[:, c * T + i * 512: c * T + (i + 1) * 512],
                             start=(c == 0), stop=(c == NCH - 1))
                    inst = t.matmul(psQ[0:1, sl], onescol[:], sq[:, sl],
                                    start=(c == 0), stop=(c == NCH - 1))
                ctx.inc(inst, "pe")
        pe_stats = ctx.tok("pe")

        if LN_LIMIT < 3:
            return 0
        negm = rowbuf[0:1, 0:T]
        rrow = rowbuf[0:1, T:2 * T]
        srow = env.get("srow_dst")
        tmpa = env["scr1"][0:1, 0:T]
        tmpb = env["scr2"][0:1, 0:T]

        @block.vector
        def _(v):
            v.wait_ge(ctx.sem("pe"), pe_stats)
            for i in range(NTT):
                sl = slice(i * 512, (i + 1) * 512)
                v.tensor_scalar_mul(negm[:, sl], psS[0:1, sl], -1.0 / D)
            v.drain()
            v.tensor_mul(tmpa, negm, negm)
            v.drain()
            for i in range(NTT):
                sl = slice(i * 512, (i + 1) * 512)
                v.scalar_tensor_tensor(tmpb[:, sl], psQ[0:1, sl], 1.0 / D,
                                       tmpa[:, sl], ALU.mult, ALU.subtract)
            ctx.incd(v, "ve")
        ve_var = ctx.tok("ve")

        if LN_LIMIT < 4:
            return 0

        @block.scalar
        def _(s):
            s.wait_ge(ctx.sem("ve"), ve_var)
            s.activation(tmpa, tmpb, AF.Sqrt, bias=env["epscol"][0:1, :])
            ctx.incd(s, "se")
        se_sqrt = ctx.tok("se")

        @block.vector
        def _(v):
            v.wait_ge(ctx.sem("se"), se_sqrt)
            v.reciprocal(rrow, tmpa)
            if want_s:
                v.reciprocal(srow, tmpa)
                v.drain()
                v.tensor_scalar_mul(srow, srow, 1.0 / 32.0)
            ctx.incd(v, "ve")
        ve_rows = ctx.tok("ve")

        @block.tensor
        def _(t):
            t.wait_ge(ctx.sem("ve"), ve_rows)

    with ExitStack() as ps:
        psA = ps.enter_context(nc.psum_tensor(ctx.uname("psLNA"), [128, T], F32))
        psB = ps.enter_context(nc.psum_tensor(ctx.uname("psLNB"), [128, T], F32))

        @block.tensor
        def _(t):
            for i in range(NTT):
                sl = slice(i * 512, (i + 1) * 512)
                t.matmul(psA[:, sl], onesrow[:], negm[:, sl], start=True, stop=True)
                inst = t.matmul(psB[:, sl], onesrow[:], rrow[:, sl], start=True, stop=True)
            ctx.inc(inst, "pe")
        pe_bcast = ctx.tok("pe")

        if LN_LIMIT < 5:
            return 0
        scr1, scr2 = env["scr1"], env["scr2"]

        @block.vector
        def _(v):
            v.wait_ge(ctx.sem("pe"), pe_bcast)
            for c in range(NCH):
                srcc = src_bf[:, c * T:(c + 1) * T]
                v.tensor_tensor(scr1[:], srcc, psA[:], ALU.add)
                if mode == "zn":
                    inst = v.tensor_tensor(zn[:, c * T:(c + 1) * T], scr1[:], psB[:], ALU.mult)
                elif mode == "gb":
                    v.tensor_tensor(scr2[:], scr1[:], psB[:], ALU.mult)
                    inst = v.tensor_scalar(zn[:, c * T:(c + 1) * T], scr2[:],
                                           gamma[:, c:c + 1], beta[:, c:c + 1], ALU.mult, ALU.add)
                else:
                    v.tensor_tensor(scr2[:], scr1[:], psB[:], ALU.mult)
                    xc = chunk(x_res, c)
                    inst = v.tensor_add(xc, xc, scr2[:])
            ctx.inc(inst, "ve")
        ve_apply = ctx.tok("ve")

        @block.tensor
        def _(t):
            t.wait_ge(ctx.sem("ve"), ve_apply)
    return ve_apply


def emit_preln(env, want_s=False):
    nc, block, ctx = env["nc"], env["block"], env["ctx"]

    @block.gpsimd
    def _(g):
        ctx.wait_compute(g)
        for c in range(NCH):
            ctx.inc(g.dma_start(out=chunk(env["znbuf"], c), in_=chunk(env["x_res"], c)), "gl", 16)
    if LN_LIMIT >= 2:
        emit_ln(env, env["znbuf"], "zn", zn=env["znbuf"], want_s=want_s)


def emit_postln(env, src):
    emit_ln(env, src, "res")


def emit_l1f(env):
    nc, block, ctx, es, dr = env["nc"], env["block"], env["ctx"], env["es"], env["dr"]
    x_res, znbuf, obuf = env["x_res"], env["znbuf"], env["obuf"]
    rowbuf, onesrow, sq = env["rowbuf"], env["onesrow"], env["sq"]

    shapes = {"cn_w1": [D, D], "cn_w2": [D, 512], "cn_w3": [512, 256], "cn_w4": [256, 1],
              "gc_w1": [D, 512], "gc_w2": [512, 256], "gc_w3": [256, 1]}
    for nm, sh in shapes.items():
        dr[nm] = nc.dram_tensor(nm, sh, F32, kind="ExternalInput").ap()
    for nm, sh in [("cn_b1", [128, 8]), ("cn_b2", [128, 4]), ("cn_b3", [128, 2]),
                   ("gc_b1", [128, 4]), ("gc_b2", [128, 2])]:
        dr[nm] = nc.dram_tensor(nm, sh, F32, kind="ExternalInput").ap()
    dr["cn_b4"] = nc.dram_tensor("cn_b4", [1, 1], F32, kind="ExternalInput").ap()
    dr["gc_b3"] = nc.dram_tensor("gc_b3", [1, 1], F32, kind="ExternalInput").ap()
    dr["arange"] = nc.dram_tensor("arange", [1, 1], F32, kind="ExternalInput").ap()

    gc_in = nc.dram_tensor("l1f_gcin", [128, 8], F32, kind="Internal")
    gc_out = nc.dram_tensor("l1f_gcout", [128, 8], F32, kind="Internal")

    if L1F_LIMIT < 1:
        return
    emit_preln(env)
    if L1F_LIMIT < 2:
        return
    if L1F_LIMIT == 2:
        @block.vector
        def _(v):
            v.wait_ge(ctx.sem("ve"), ctx.tok("ve"))
            for c in range(NCH):
                inst = v.tensor_copy(chunk(x_res, c), chunk(znbuf, c))
            ctx.inc(inst, "ve")
        return

    with ExitStack() as ss:
        bcols = ss.enter_context(nc.sbuf_tensor("l1f_b", [128, 16], F32))
        brow = ss.enter_context(nc.sbuf_tensor("l1f_br", [1, 8], F32))
        gcv = ss.enter_context(nc.sbuf_tensor("l1f_gc", [128, 24], F32))
        gcb = ss.enter_context(nc.sbuf_tensor("l1f_gcb", [128, 16], BF16))
        tfrow = rowbuf[0:1, 0:512]

        @block.sync
        def _(y):
            for src, dst in [(dr["cn_b1"], bcols[:, 0:8]), (dr["cn_b2"], bcols[:, 8:12]),
                             (dr["cn_b3"], bcols[:, 12:14]), (dr["gc_b1"], gcv[:, 8:12]),
                             (dr["gc_b2"], gcv[:, 12:14]), (dr["cn_b4"], brow[:, 0:1]),
                             (dr["gc_b3"], brow[:, 1:2]), (dr["arange"], brow[:, 2:3])]:
                ctx.inc(y.dma_start(out=dst, in_=src), "ld", 16)
        ld_b = ctx.tok("ld")

        if L1F_LIMIT < 3:
            return
        @block.vector
        def _(v):
            v.wait_ge(ctx.sem("ve"), ctx.tok("ve"))
            for c in range(NCH):
                v.reduce_sum(gcv[:, c:c + 1], chunk(znbuf, c), axis=AX.X)
            ctx.incd(v, "ve")
        ve_gc = ctx.tok("ve")

        @block.gpsimd
        def _(g):
            g.wait_ge(ctx.sem("ve"), ve_gc)
            ctx.inc(g.dma_start(out=gc_in[:, :], in_=gcv[:, 0:8]), "gl", 16)
            g.wait_ge(ctx.sem("gl"), ctx.tok("gl"))
            ctx.inc(g.collective_compute("AllReduce", ALU.add, replica_groups=PAIRS,
                                         ins=[gc_in.ap()], outs=[gc_out.ap()]), "cc", 1)
            g.wait_ge(ctx.sem("cc"), ctx.tok("cc"))
            ctx.inc(g.dma_start(out=gcv[:, 16:24], in_=gc_out.ap()), "gl", 16)
        gl_gc = ctx.tok("gl")

        if L1F_LIMIT < 4:
            return
        with ExitStack() as gs:
            g1 = gs.enter_context(nc.sbuf_tensor("l1f_g1", [128, 8 * 512], BF16))
            g2 = gs.enter_context(nc.sbuf_tensor("l1f_g2", [128, 4 * 256], BF16))
            g3 = gs.enter_context(nc.sbuf_tensor("l1f_g3", [128, 2], BF16))
            gin = [gs.enter_context(nc.sbuf_tensor(ctx.uname("l1f_gin"), [128, 1], BF16))
                   for _ in range(8)]
            gh = [gs.enter_context(nc.sbuf_tensor(ctx.uname("l1f_gh"), [128, 1], BF16))
                  for _ in range(6)]
            gb = [gs.enter_context(nc.sbuf_tensor(ctx.uname("l1f_gb"), [128, 1], F32))
                  for _ in range(6)]
            psg = [gs.enter_context(nc.psum_tensor(ctx.uname("l1f_psg"), [128, 1], F32))
                   for _ in range(7)]

            @block.gpsimd
            def _(g):
                for c in range(8):
                    ctx.inc(g.dma_start(out=g1[:, c * 512:(c + 1) * 512],
                                        in_=dr["gc_w1"][c * 128:(c + 1) * 128, :]), "wt", 16)
                for c in range(4):
                    ctx.inc(g.dma_start(out=g2[:, c * 256:(c + 1) * 256],
                                        in_=dr["gc_w2"][c * 128:(c + 1) * 128, :]), "wt", 16)
                for c in range(2):
                    ctx.inc(g.dma_start(out=g3[:, c:c + 1],
                                        in_=dr["gc_w3"][c * 128:(c + 1) * 128, :]), "wt", 16)
            wt_g = ctx.tok("wt")

            @block.sync
            def _(y):
                for hc in range(4):
                    ctx.inc(y.dma_start(out=gb[hc][:], in_=dr["gc_b1"][:, hc:hc + 1],
                                        allow_slow_non_contiguous=True), "ld", 16)
                for hc in range(2):
                    ctx.inc(y.dma_start(out=gb[4 + hc][:], in_=dr["gc_b2"][:, hc:hc + 1],
                                        allow_slow_non_contiguous=True), "ld", 16)
            ld_gb = ctx.tok("ld")

            @block.vector
            def _(v):
                v.wait_ge(ctx.sem("gl"), gl_gc)
                v.wait_ge(ctx.sem("ld"), ld_b)
                for c in range(8):
                    v.tensor_scalar_mul(gin[c][:], gcv[:, 16 + c:17 + c], 1.0 / 4096.0)
                if os.environ.get("L1F_DBG") == "gc":
                    v.tensor_scalar_mul(chunk(x_res, 0)[:, 0:8], gcv[:, 16:24], 1.0 / 4096.0)
                ctx.incd(v, "ve")
            ve_gcs = ctx.tok("ve")

            @block.tensor
            def _(t):
                t.wait_ge(ctx.sem("ve"), ve_gcs)
                t.wait_ge(ctx.sem("wt"), wt_g)
                for hc in range(4):
                    for c in range(8):
                        t.matmul(psg[hc][:],
                                 g1[:, c * 512 + hc * 128: c * 512 + (hc + 1) * 128],
                                 gin[c][:], start=(c == 0), stop=(c == 7))
                ctx.incd(t, "pe")
            pe_g1 = ctx.tok("pe")

            @block.scalar
            def _(s):
                s.wait_ge(ctx.sem("pe"), pe_g1)
                s.wait_ge(ctx.sem("ld"), ld_gb)
                for hc in range(4):
                    s.activation(gh[hc][:], psg[hc][:], AF.Gelu, bias=gb[hc][:])
                ctx.incd(s, "se")
            se_g1 = ctx.tok("se")

            @block.tensor
            def _(t):
                t.wait_ge(ctx.sem("se"), se_g1)
                for hc in range(2):
                    for c in range(4):
                        t.matmul(psg[4 + hc][:],
                                 g2[:, c * 256 + hc * 128: c * 256 + (hc + 1) * 128],
                                 gh[c][:], start=(c == 0), stop=(c == 3))
                ctx.incd(t, "pe")
            pe_g2 = ctx.tok("pe")

            @block.scalar
            def _(s):
                s.wait_ge(ctx.sem("pe"), pe_g2)
                for hc in range(2):
                    s.activation(gh[4 + hc][:], psg[4 + hc][:], AF.Gelu,
                                 bias=gb[4 + hc][:])
                ctx.incd(s, "se")
            se_g2 = ctx.tok("se")

            @block.tensor
            def _(t):
                t.wait_ge(ctx.sem("se"), se_g2)
                for c in range(2):
                    t.matmul(psg[6][0:1, :], g3[:, c:c + 1], gh[4 + c][:],
                             start=(c == 0), stop=(c == 1))
                ctx.incd(t, "pe")
            pe_g3 = ctx.tok("pe")

            @block.scalar
            def _(s):
                s.wait_ge(ctx.sem("pe"), pe_g3)
                s.activation(brow[:, 3:4], psg[6][0:1, :], AF.Sigmoid,
                             bias=brow[:, 1:2])
                ctx.incd(s, "se")
            se_gf = ctx.tok("se")

            @block.vector
            def _(v):
                v.wait_ge(ctx.sem("se"), se_gf)
                v.tensor_scalar(brow[:, 4:5], brow[:, 3:4], 0.7, -0.5, ALU.mult, ALU.add)
                if os.environ.get("L1F_DBG") == "gf":
                    v.tensor_copy(chunk(x_res, 0)[0:1, 0:8], brow[:, 0:8])
                ctx.incd(v, "ve")

        if L1F_LIMIT < 5:
            return
        with ExitStack() as cs:
            w1 = cs.enter_context(nc.sbuf_tensor("l1f_w1", [128, 8 * 1024], BF16))
            w2 = cs.enter_context(nc.sbuf_tensor("l1f_w2", [128, 8 * 512], BF16))
            w3 = cs.enter_context(nc.sbuf_tensor("l1f_w3", [128, 4 * 256], BF16))
            w4 = cs.enter_context(nc.sbuf_tensor("l1f_w4", [128, 2], BF16))
            h1 = cs.enter_context(nc.sbuf_tensor("l1f_h1", [128, 8 * 512], BF16))
            h2 = cs.enter_context(nc.sbuf_tensor("l1f_h2", [128, 4 * 512], BF16))
            h3 = cs.enter_context(nc.sbuf_tensor("l1f_h3", [128, 2 * 512], BF16))
            tw = env["scr1"][:].bitcast(BF16)
            psh = cs.enter_context(nc.psum_tensor("l1f_psh", [128, 512], F32))
            psc = cs.enter_context(nc.psum_tensor("l1f_psc", [128, 512], F32))

            @block.gpsimd
            def _(g):
                ctx.wait_compute(g)
                for c in range(8):
                    ctx.inc(g.dma_start(out=w1[:, c * 1024:(c + 1) * 1024],
                                        in_=dr["cn_w1"][c * 128:(c + 1) * 128, :]), "wt", 16)
                    ctx.inc(g.dma_start(out=w2[:, c * 512:(c + 1) * 512],
                                        in_=dr["cn_w2"][c * 128:(c + 1) * 128, :]), "wt", 16)
                for c in range(4):
                    ctx.inc(g.dma_start(out=w3[:, c * 256:(c + 1) * 256],
                                        in_=dr["cn_w3"][c * 128:(c + 1) * 128, :]), "wt", 16)
                for c in range(2):
                    ctx.inc(g.dma_start(out=w4[:, c:c + 1],
                                        in_=dr["cn_w4"][c * 128:(c + 1) * 128, :]), "wt", 16)
            wt_all = ctx.tok("wt")
            first_gate = True
            for i in range(NTT):
                tsl = slice(i * 512, (i + 1) * 512)
                for cyc in range(int(os.environ.get("L1F_CYCLES", "3"))):
                    for dst, wsb, hsrc, hw, nh, nsrc, bofs in (
                            (h1, w1, znbuf, T, 8, 8, 0),
                            (h2, w2, h1, 512, 4, 8, 8),
                            (h3, w3, h2, 512, 2, 4, 12)):
                        for hc in range(nh):
                            @block.tensor
                            def _(t, dst=dst, wsb=wsb, hsrc=hsrc, hw=hw, nh=nh, nsrc=nsrc,
                                  hc=hc, i=i, fg=first_gate):
                                if fg:
                                    t.wait_ge(ctx.sem("wt"), wt_all)
                                    t.wait_ge(ctx.sem("ve"), ctx.tok("ve"))
                                for c in range(nsrc):
                                    if hsrc is znbuf:
                                        rhs = hsrc[:, c * hw + i * 512: c * hw + (i + 1) * 512]
                                    else:
                                        rhs = hsrc[:, c * hw:(c + 1) * hw]
                                    inst = t.matmul(psh[:, 0:512],
                                                    wsb[:, c * (nh * 128) + hc * 128: c * (nh * 128) + (hc + 1) * 128],
                                                    rhs, start=(c == 0), stop=(c == nsrc - 1))
                                ctx.inc(inst, "pe")
                            first_gate = False
                            tk = ctx.tok("pe")

                            @block.scalar
                            def _(s, dst=dst, hc=hc, tk=tk, bofs=bofs):
                                s.wait_ge(ctx.sem("pe"), tk)
                                ctx.inc(s.activation(dst[:, hc * 512:(hc + 1) * 512],
                                                     psh[:, 0:512], AF.Gelu,
                                                     bias=bcols[:, bofs + hc:bofs + hc + 1]), "se")
                            tk2 = ctx.tok("se")

                            @block.tensor
                            def _(t, tk2=tk2):
                                t.wait_ge(ctx.sem("se"), tk2)

                    if os.environ.get("L1F_DBG") == "h1":
                        @block.vector
                        def _(v, i=i):
                            v.wait_ge(ctx.sem("se"), ctx.tok("se"))
                            inst = v.tensor_copy(chunk(x_res, 0)[:, i * 512:(i + 1) * 512],
                                                 h1[:, 0:512])
                            ctx.inc(inst, "ve")
                        continue
                    @block.tensor
                    def _(t):
                        for c in range(2):
                            inst = t.matmul(psc[0:1, 0:512], w4[:, c:c + 1],
                                            h3[:, c * 512:(c + 1) * 512],
                                            start=(c == 0), stop=(c == 1))
                        ctx.inc(inst, "pe")
                    pe_tf = ctx.tok("pe")

                    @block.scalar
                    def _(s):
                        s.wait_ge(ctx.sem("pe"), pe_tf)
                        s.activation(tfrow, psc[0:1, 0:512], AF.Sigmoid,
                                     bias=brow[:, 0:1])
                        ctx.incd(s, "se")
                    se_tf = ctx.tok("se")

                    @block.vector
                    def _(v):
                        v.wait_ge(ctx.sem("se"), se_tf)
                        if os.environ.get("L1F_DBG") == "tf":
                            ctx.inc(v.tensor_copy(rowbuf[0:1, 512:1024], tfrow), "ve")
                            return
                        v.tensor_scalar(tfrow, tfrow, 0.3, None, ALU.mult)
                        v.drain()
                        v.tensor_scalar_add(tfrow, tfrow, brow[:, 4:5])
                        v.drain()
                        v.tensor_scalar(tfrow, tfrow, brow[:, 2:3], None, ALU.mult)
                        v.drain()
                        v.tensor_scalar(tfrow, tfrow, 2.0, 0.1, ALU.mult, ALU.add)
                        ctx.incd(v, "ve")
                    ve_cp = ctx.tok("ve")

                    @block.tensor
                    def _(t):
                        t.wait_ge(ctx.sem("ve"), ve_cp)
                        inst = t.matmul(psc[:, 0:512], onesrow[:], tfrow, start=True, stop=True)
                        ctx.inc(inst, "pe")
                    pe_cb = ctx.tok("pe")

                    @block.vector
                    def _(v, i=i):
                        tsl2 = slice(i * 512, (i + 1) * 512)
                        v.wait_ge(ctx.sem("pe"), pe_cb)
                        if os.environ.get("L1F_DBG") in ("cb", "tf"):
                            inst = v.tensor_copy(chunk(x_res, 0)[:, tsl2], psc[:, 0:512])
                            ctx.inc(inst, "ve")
                            return
                        for c in range(4):
                            a = chunk(znbuf, c)[:, tsl2]
                            b2 = chunk(znbuf, c + 4)[:, tsl2]
                            opa, opb = (ALU.add, ALU.subtract) if c < 2 else (ALU.subtract, ALU.add)
                            v.tensor_tensor(tw[:, 0:512], a, psc[:, 0:512], ALU.mult)
                            v.tensor_tensor(tw[:, 512:1024], b2, psc[:, 0:512], ALU.mult)
                            v.tensor_tensor(a, a, tw[:, 512:1024], opa)
                            inst = v.tensor_tensor(b2, b2, tw[:, 0:512], opb)
                        ctx.inc(inst, "ve")
                    ve_tw = ctx.tok("ve")

                    @block.tensor
                    def _(t, tk=ve_tw):
                        t.wait_ge(ctx.sem("ve"), tk)

    if L1F_LIMIT < 6:
        if not os.environ.get("L1F_DBG"):
            @block.vector
            def _(v):
                v.wait_ge(ctx.sem("ve"), ctx.tok("ve"))
                for c in range(NCH):
                    inst = v.tensor_copy(chunk(x_res, c), chunk(znbuf, c))
                ctx.inc(inst, "ve")
        return
    emit_postln(env, znbuf)



def emit_l0f(env):
    nc, block, ctx, es, dr = env["nc"], env["block"], env["ctx"], env["es"], env["dr"]
    x_res, znbuf, obuf = env["x_res"], env["znbuf"], env["obuf"]
    rowbuf, onesrow, onescol, ident = env["rowbuf"], env["onesrow"], env["onescol"], env["ident"]

    dr["gate_wT"] = nc.dram_tensor("gate_wT", [1024, 16], F32, kind="ExternalInput").ap()
    dr["w_gate"] = nc.dram_tensor("w_gate", [16, 1024, 512], F32, kind="ExternalInput").ap()
    dr["w_up"] = nc.dram_tensor("w_up", [16, 1024, 512], F32, kind="ExternalInput").ap()
    dr["w_down"] = nc.dram_tensor("w_down", [16, 512, 1024], F32, kind="ExternalInput").ap()
    dr["sw_gate"] = nc.dram_tensor("sw_gate", [1024, 512], F32, kind="ExternalInput").ap()
    dr["sw_up"] = nc.dram_tensor("sw_up", [1024, 512], F32, kind="ExternalInput").ap()
    dr["sw_down"] = nc.dram_tensor("sw_down", [512, 1024], F32, kind="ExternalInput").ap()

    emit_preln(env)

    with ExitStack() as ss:
        gwt = ss.enter_context(nc.sbuf_tensor("l0f_gwt", [128, 8 * 16], BF16))
        sqg = ss.enter_context(nc.sbuf_tensor("l0f_sqg", [128, 8 * 16], BF16))
        row16 = ss.enter_context(nc.sbuf_tensor("l0f_row16", [1, 16], F32))
        gwr = ss.enter_context(nc.sbuf_tensor("l0f_gwr", [16, 1], F32))
        lsc = ss.enter_context(nc.sbuf_tensor("l0f_lsc", [16, T], F32))
        stok = ss.enter_context(nc.sbuf_tensor("l0f_stok", [128, 256], F32))
        acc = ss.enter_context(nc.sbuf_tensor("l0f_acc", [128, 256], F32))
        tmpc = ss.enter_context(nc.sbuf_tensor("l0f_tmpc", [128, 256], F32))
        wsum = ss.enter_context(nc.sbuf_tensor("l0f_wsum", [128, 16], F32))
        ctok = ss.enter_context(nc.sbuf_tensor("l0f_ctok", [128, 256], F32))
        onecol1 = ss.enter_context(nc.sbuf_tensor("l0f_one1", [1, 1], F32))

        @block.gpsimd
        def _(g):
            ctx.inc(g.dma_start(out=gwt[:].rearrange("p (c e) -> p c e", e=16),
                                in_=dr["gate_wT"].rearrange("(c p) e -> p c e", p=128)), "wt", 16)
        wt_gwt = ctx.tok("wt")

        @block.vector
        def _(v):
            v.memset(onecol1[:], 1.0)
            ctx.incd(v, "ve")

        @block.scalar
        def _(s):
            s.wait_ge(ctx.sem("wt"), wt_gwt)
            s.activation(sqg[:], gwt[:], AF.Square)
            ctx.incd(s, "se")
        se_sqg = ctx.tok("se")

        with ExitStack() as ps:
            psq = ps.enter_context(nc.psum_tensor("l0f_psq", [1, 16], F32))
            psc16 = ps.enter_context(nc.psum_tensor("l0f_psc16", [16, 1], F32))
            pl = ps.enter_context(nc.psum_tensor("l0f_pl", [16, 512], F32))
            psTk = ps.enter_context(nc.psum_tensor("l0f_psTk", [128, 64], F32))

            @block.tensor
            def _(t):
                t.wait_ge(ctx.sem("se"), se_sqg)
                for c in range(NCH):
                    t.matmul(psq[:], onescol[:], sqg[:, c * 16:(c + 1) * 16],
                             start=(c == 0), stop=(c == NCH - 1))
                ctx.incd(t, "pe")
            pe_gw = ctx.tok("pe")

            @block.vector
            def _(v):
                v.wait_ge(ctx.sem("pe"), pe_gw)
                v.tensor_copy(row16[:], psq[:])
                ctx.incd(v, "ve")
            ve_r16 = ctx.tok("ve")

            @block.tensor
            def _(t):
                t.wait_ge(ctx.sem("ve"), ve_r16)
                t.matmul(psc16[:], row16[:], onecol1[:], start=True, stop=True)
                ctx.incd(t, "pe")
            pe_c16 = ctx.tok("pe")

            @block.scalar
            def _(s):
                s.wait_ge(ctx.sem("pe"), pe_c16)
                s.activation(gwr[:], psc16[:], AF.Sqrt)
                ctx.incd(s, "se")
            se_gn = ctx.tok("se")

            @block.vector
            def _(v):
                v.wait_ge(ctx.sem("se"), se_gn)
                v.reciprocal(gwr[:], gwr[:])
                v.drain()
                v.tensor_scalar_mul(gwr[:], gwr[:], 1.0 / 32.0)
                ctx.incd(v, "ve")
            ve_gwr = ctx.tok("ve")

            for tt in range(NTT):
                sl = slice(tt * 512, (tt + 1) * 512)

                @block.tensor
                def _(t, tt=tt, sl=sl):
                    if tt == 0:
                        t.wait_ge(ctx.sem("ve"), ve_gwr)
                    for c in range(NCH):
                        t.matmul(pl[:], gwt[:, c * 16:(c + 1) * 16],
                                 znbuf[:, c * T + tt * 512: c * T + (tt + 1) * 512],
                                 start=(c == 0), stop=(c == NCH - 1))
                    ctx.incd(t, "pe")
                pe_l = ctx.tok("pe")

                @block.vector
                def _(v, tt=tt, tk=pe_l):
                    v.wait_ge(ctx.sem("pe"), tk)
                    v.tensor_scalar(env["scr1"][0:16, 0:512], pl[:], gwr[:, 0:1], None, ALU.mult)
                    ctx.incd(v, "ve")
                ve_l = ctx.tok("ve")

                @block.scalar
                def _(s, tt=tt, tk=ve_l, sl=sl):
                    s.wait_ge(ctx.sem("ve"), tk)
                    s.activation(lsc[:, sl], env["scr1"][0:16, 0:512], AF.Sigmoid)
                    ctx.incd(s, "se")
                se_sc = ctx.tok("se")

                @block.tensor
                def _(t, tt=tt, tk=se_sc):
                    t.wait_ge(ctx.sem("se"), tk)
                    for j in range(4):
                        t.transpose(psTk[:, j * 16:(j + 1) * 16],
                                    lsc[0:16, tt * 512 + j * 128: tt * 512 + (j + 1) * 128],
                                    env["ident32"][0:16, 0:16])
                    ctx.incd(t, "pe")
                pe_tk = ctx.tok("pe")

                @block.vector
                def _(v, tt=tt, tk=pe_tk):
                    v.wait_ge(ctx.sem("pe"), tk)
                    v.tensor_copy(stok[:, tt * 64:(tt + 1) * 64], psTk[:, 0:64])
                    ctx.incd(v, "ve")
            ve_stok = ctx.tok("ve")

            @block.vector
            def _(v):
                v.wait_ge(ctx.sem("ve"), ve_stok)
                v.memset(acc[:], 0.0)
                sv = stok[:].rearrange("p (g e) -> p g e", e=16)
                av = acc[:].rearrange("p (g e) -> p g e", e=16)
                tv = tmpc[:].rearrange("p (g e) -> p g e", e=16)
                for r in range(1, 16):
                    n1 = 16 - r
                    v.tensor_tensor(tv[:, :, 0:n1], sv[:, :, r:16], sv[:, :, 0:n1], ALU.is_gt)
                    v.tensor_tensor(tv[:, :, n1:16], sv[:, :, 0:r], sv[:, :, n1:16], ALU.is_gt)
                    v.drain()
                    v.tensor_tensor(acc[:], acc[:], tmpc[:], ALU.add)
                    v.drain()
                    v.tensor_tensor(tv[:, :, r:16], sv[:, :, 0:n1], sv[:, :, r:16], ALU.is_equal)
                    v.drain()
                    v.tensor_tensor(av[:, :, r:16], av[:, :, r:16], tv[:, :, r:16], ALU.add)
                    v.drain()
                    v.tensor_tensor(tv[:, :, n1:16], sv[:, :, 0:r], sv[:, :, n1:16], ALU.is_equal)
                    v.drain()
                    v.tensor_tensor(av[:, :, n1:16], av[:, :, n1:16], tv[:, :, n1:16], ALU.add)
                    v.drain()
                v.tensor_scalar(tmpc[:], acc[:], 7.5, None, ALU.is_lt)
                v.drain()
                v.tensor_mul(ctok[:], stok[:], tmpc[:])
                v.drain()
                v.reduce_sum(wsum[:], ctok[:].rearrange("p (g e) -> p g e", e=16), axis=AX.X)
                v.drain()
                v.reciprocal(wsum[:], wsum[:])
                v.drain()
                v.tensor_tensor(ctok[:].rearrange("p (g e) -> p g e", e=16),
                                ctok[:].rearrange("p (g e) -> p g e", e=16),
                                wsum[:].rearrange("p (g o) -> p g o", o=1).broadcast_to([128, 16, 16]),
                                ALU.mult)
                ctx.incd(v, "ve")
            ve_ctok = ctx.tok("ve")

        wg = ss.enter_context(nc.sbuf_tensor("l0f_wg", [128, 8 * 512], BF16))
        wu = ss.enter_context(nc.sbuf_tensor("l0f_wu", [128, 8 * 512], BF16))
        wd = ss.enter_context(nc.sbuf_tensor("l0f_wd", [128, 4 * 1024], BF16))
        sg = ss.enter_context(nc.sbuf_tensor("l0f_sg", [128, 512], BF16))
        t1tok = ss.enter_context(nc.sbuf_tensor("l0f_t1tok", [128, 512], BF16))
        t1T = ss.enter_context(nc.sbuf_tensor("l0f_t1T", [128, 4 * 256], BF16))

        with ExitStack() as ps:
            routedps = ps.enter_context(nc.psum_tensor("l0f_routed", [128, 8 * 256], F32))
            psG = ps.enter_context(nc.psum_tensor("l0f_psG", [128, 512], F32))
            psU = ps.enter_context(nc.psum_tensor("l0f_psU", [128, 512], F32))
            psT = ps.enter_context(nc.psum_tensor("l0f_psT", [128, 512], BF16))

            items = [("e", i) for i in range(16)] + [("sh", 0)]
            se_evac = ctx.tok("se")
            pe_down_last = ctx.tok("pe")
            ve_t1T = ctx.tok("ve")
            se_silu = ctx.tok("se")
            ve_stt = ctx.tok("ve")

            for tt in range(8):
                for ii, (kind, ei) in enumerate(items):
                    @block.gpsimd
                    def _(g, kind=kind, ei=ei, gate_pe=pe_down_last):
                        g.wait_ge(ctx.sem("pe"), gate_pe)
                        if kind == "e":
                            ctx.inc(g.dma_start(out=wg[:].rearrange("p (c h) -> p c h", h=512), in_=dr["w_gate"][ei].rearrange("(c p) h -> p c h", p=128)), "wt", 16)
                            ctx.inc(g.dma_start(out=wu[:].rearrange("p (c h) -> p c h", h=512), in_=dr["w_up"][ei].rearrange("(c p) h -> p c h", p=128)), "wt", 16)
                            ctx.inc(g.dma_start(out=wd[:].rearrange("p (c d) -> p c d", d=1024), in_=dr["w_down"][ei].rearrange("(c p) d -> p c d", p=128)), "wt", 16)
                        else:
                            ctx.inc(g.dma_start(out=wg[:].rearrange("p (c h) -> p c h", h=512), in_=dr["sw_gate"].rearrange("(c p) h -> p c h", p=128)), "wt", 16)
                            ctx.inc(g.dma_start(out=wu[:].rearrange("p (c h) -> p c h", h=512), in_=dr["sw_up"].rearrange("(c p) h -> p c h", p=128)), "wt", 16)
                            ctx.inc(g.dma_start(out=wd[:].rearrange("p (c d) -> p c d", d=1024), in_=dr["sw_down"].rearrange("(c p) d -> p c d", p=128)), "wt", 16)
                    wt_e = ctx.tok("wt")

                    for sub in range(2):
                        st_idx = tt * 2 + sub
                        tcol = slice(st_idx * 128, (st_idx + 1) * 128)

                        @block.tensor
                        def _(t, wt_e=wt_e, tcol=tcol, g_se=se_silu, g_ve=ve_stt):
                            t.wait_ge(ctx.sem("wt"), wt_e)
                            t.wait_ge(ctx.sem("se"), g_se)
                            for c in range(NCH):
                                t.matmul(psG[:], znbuf[:, c * T:(c + 1) * T][:, tcol],
                                         wg[:, c * 512:(c + 1) * 512],
                                         start=(c == 0), stop=(c == NCH - 1))
                            ctx.incd(t, "pe")
                        pe_g = ctx.tok("pe")

                        @block.scalar
                        def _(s, tk=pe_g):
                            s.wait_ge(ctx.sem("pe"), tk)
                            s.activation(sg[:], psG[:], AF.Silu)
                            ctx.incd(s, "se")
                        se_silu = ctx.tok("se")

                        @block.tensor
                        def _(t, tcol=tcol, g_ve=ve_stt):
                            t.wait_ge(ctx.sem("ve"), g_ve)
                            for c in range(NCH):
                                t.matmul(psU[:], znbuf[:, c * T:(c + 1) * T][:, tcol],
                                         wu[:, c * 512:(c + 1) * 512],
                                         start=(c == 0), stop=(c == NCH - 1))
                            ctx.incd(t, "pe")
                        pe_u = ctx.tok("pe")

                        @block.vector
                        def _(v, tk=pe_u, tks=se_silu, kind=kind, ei=ei, st_idx=st_idx):
                            v.wait_ge(ctx.sem("pe"), tk)
                            v.wait_ge(ctx.sem("se"), tks)
                            if kind == "e":
                                v.scalar_tensor_tensor(t1tok[:], psU[:],
                                                       ctok[:, st_idx * 16 + ei: st_idx * 16 + ei + 1],
                                                       sg[:], ALU.mult, ALU.mult)
                            else:
                                v.scalar_tensor_tensor(t1tok[:], psU[:], 1.0, sg[:],
                                                       ALU.mult, ALU.mult)
                            ctx.incd(v, "ve")
                        ve_stt = ctx.tok("ve")

                        @block.tensor
                        def _(t, tk=ve_stt, g_ve=ve_t1T):
                            t.wait_ge(ctx.sem("ve"), tk)
                            for hc in range(4):
                                t.transpose(psT[:, hc * 128:(hc + 1) * 128],
                                            t1tok[:, hc * 128:(hc + 1) * 128], ident[:])
                            ctx.incd(t, "pe")
                        pe_tr = ctx.tok("pe")

                        @block.vector
                        def _(v, tk=pe_tr, sub=sub):
                            v.wait_ge(ctx.sem("pe"), tk)
                            v.tensor_copy(t1T[:].rearrange("p (hc t) -> p hc t", t=256)[:, :, sub * 128:(sub + 1) * 128],
                                          psT[:].rearrange("p (hc t) -> p hc t", t=128))
                            ctx.incd(v, "ve")
                        ve_t1T = ctx.tok("ve")

                    @block.tensor
                    def _(t, ii=ii, tk=ve_t1T, tks=se_evac):
                        t.wait_ge(ctx.sem("ve"), tk)
                        if ii == 0:
                            t.wait_ge(ctx.sem("se"), tks)
                        for k in range(NCH):
                            for hc in range(4):
                                t.matmul(routedps[:, k * 256:(k + 1) * 256],
                                         wd[:, hc * 1024 + k * 128: hc * 1024 + (k + 1) * 128],
                                         t1T[:, hc * 256:(hc + 1) * 256],
                                         start=(ii == 0 and hc == 0 and k % 2 == 0),
                                         stop=(ii == len(items) - 1 and hc == 3 and k % 2 == 1))
                        ctx.incd(t, "pe")
                    pe_down_last = ctx.tok("pe")

                @block.scalar
                def _(s, tt=tt, tk=pe_down_last):
                    s.wait_ge(ctx.sem("pe"), tk)
                    for k in range(NCH):
                        s.copy(obuf[:, k * T + tt * 256: k * T + (tt + 1) * 256],
                               routedps[:, k * 256:(k + 1) * 256])
                    ctx.incd(s, "se")
                se_evac = ctx.tok("se")

    emit_postln(env, obuf)


def _l0f_inputs(inputs):
    d = {}
    d["gate_wT"] = np.ascontiguousarray(np.asarray(inputs["L0_ffn_gate_w"], np.float32).T)
    for nm in ("w_gate", "w_up", "w_down", "sw_gate", "sw_up", "sw_down"):
        d[nm] = np.ascontiguousarray(np.asarray(inputs["L0_ffn_" + nm], np.float32))
    return d



def emit_l1a(env):
    nc, block, ctx, es, dr = env["nc"], env["block"], env["ctx"], env["es"], env["dr"]
    x_res, znbuf, obuf = env["x_res"], env["znbuf"], env["obuf"]
    rowbuf, onesrow, onescol, onesrow_bf = env["rowbuf"], env["onesrow"], env["onescol"], env["onesrow_bf"]
    scr1, scr2, sq = env["scr1"], env["scr2"], env["sq"]
    onesall, onescol32 = env["onesall"], env["onescol32"]

    for n in "qkvo":
        dr[f"W{n}T"] = nc.dram_tensor(f"W{n}T", [D, D], F32, kind="ExternalInput").ap()
        dr[f"A{n}32"] = nc.dram_tensor(f"A{n}32", [32, D], F32, kind="ExternalInput").ap()
        dr[f"B{n}T32"] = nc.dram_tensor(f"B{n}T32", [32, D], F32, kind="ExternalInput").ap()
        dr[f"b{n}"] = nc.dram_tensor(f"b{n}", [1, D], F32, kind="ExternalInput").ap()

    kin = nc.dram_tensor("l1a_kin", [1024, T], BF16, kind="Internal")
    kout = nc.dram_tensor("l1a_kout", [2048, T], BF16, kind="Internal")
    vin = nc.dram_tensor("l1a_vin", [T, 1024], BF16, kind="Internal")
    vout = nc.dram_tensor("l1a_vout", [2 * T, 1024], BF16, kind="Internal")

    emit_preln(env)
    if L1A_LIMIT < 2:
        return

    with ExitStack() as ss:
        wT = ss.enter_context(nc.sbuf_tensor("l1a_wT", [128, 8 * 512], BF16))
        a32 = ss.enter_context(nc.sbuf_tensor("l1a_a32", [32, D], BF16))
        b32 = ss.enter_context(nc.sbuf_tensor("l1a_b32", [32, D], BF16))
        brow = ss.enter_context(nc.sbuf_tensor("l1a_brow", [1, 1024], BF16))
        qpair = ss.enter_context(nc.sbuf_tensor("l1a_qpair", [128, T], BF16))
        q65 = ss.enter_context(nc.sbuf_tensor("l1a_q65", [128, T], BF16))
        K65 = ss.enter_context(nc.sbuf_tensor("l1a_K65", [128, 2 * T], BF16))
        vbuf = ss.enter_context(nc.sbuf_tensor("l1a_vbuf", [128, 32 * 65], BF16))
        kscope = ExitStack()
        ktmp = kscope.enter_context(nc.sbuf_tensor("l1a_ktmp", [128, T], BF16))

        with ExitStack() as ps:
            ppj = ps.enter_context(nc.psum_tensor("l1a_ppj", [128, 512], F32))
            psF = ps.enter_context(nc.psum_tensor("l1a_psF", [128, 512], F32))

            def load_ab(n, gate_sems):
                @block.gpsimd
                def _(g):
                    for snm, tok in gate_sems:
                        g.wait_ge(ctx.sem(snm), tok)
                    ctx.inc(g.dma_start(out=a32[:], in_=dr[f"A{n}32"]), "wt", 16)
                    ctx.inc(g.dma_start(out=b32[:], in_=dr[f"B{n}T32"]), "wt", 16)
                    ctx.inc(g.dma_start(out=brow[:], in_=dr[f"b{n}"]), "wt", 16)
                return ctx.tok("wt")

            def load_fold(n, wcols, dst_cols, gate_pe=True):
                width = wcols.stop - wcols.start

                @block.gpsimd
                def _(g):
                    g.wait_ge(ctx.sem("pe"), ctx.tok("pe"))
                    ctx.inc(g.dma_start(
                        out=wT[:, dst_cols].rearrange("p (cc e) -> p cc e", e=width),
                        in_=dr[f"W{n}T"][:, wcols].rearrange("(cc p) e -> p cc e", p=128)), "wt", 16)
                wtk = ctx.tok("wt")

                for cc in range(NCH):
                    @block.tensor
                    def _(t, cc=cc, wtk=wtk):
                        t.wait_ge(ctx.sem("wt"), wtk)
                        t.wait_ge(ctx.sem("ve"), ctx.tok("ve"))
                        t.matmul(psF[:, 0:width], a32[:, cc * 128:(cc + 1) * 128],
                                 b32[:, wcols], start=True, stop=True)
                        ctx.incd(t, "pe")
                    tk = ctx.tok("pe")

                    @block.vector
                    def _(v, cc=cc, tk=tk):
                        v.wait_ge(ctx.sem("pe"), tk)
                        dsl = slice(dst_cols.start + cc * width, dst_cols.start + (cc + 1) * width)
                        v.scalar_tensor_tensor(wT[:, dsl], psF[:, 0:width], 2.0,
                                               wT[:, dsl], ALU.mult, ALU.add)
                        ctx.incd(v, "ve")
                return ctx.tok("ve")

            wt_bk = load_ab("k", [("pe", ctx.tok("pe"))])
            for c in range(NCH):
                ve_w = load_fold("k", slice(c * 128, (c + 1) * 128), slice(0, 1024))
                for tt in range(NTT):
                    @block.tensor
                    def _(t, c=c, tt=tt, ve_w=ve_w, wt_bk=wt_bk):
                        t.wait_ge(ctx.sem("ve"), ve_w)
                        t.wait_ge(ctx.sem("wt"), wt_bk)
                        t.wait_ge(ctx.sem("se"), ctx.tok("se"))
                        for cc in range(NCH):
                            t.matmul(ppj[:, 0:512],
                                     wT[:, cc * 128:(cc + 1) * 128],
                                     znbuf[:, cc * T + tt * 512: cc * T + (tt + 1) * 512],
                                     start=(cc == 0), stop=False)
                        t.matmul(ppj[:, 0:512], brow[:, c * 128:(c + 1) * 128],
                                 onesrow_bf[:, 0:512], start=False, stop=True)
                        ctx.incd(t, "pe")
                    tk = ctx.tok("pe")

                    @block.scalar
                    def _(sx, tt=tt, tk=tk):
                        sx.wait_ge(ctx.sem("pe"), tk)
                        sx.copy(ktmp[:, tt * 512:(tt + 1) * 512], ppj[:, 0:512])
                        ctx.incd(sx, "se")
                se_kev = ctx.tok("se")

                @block.sync
                def _(y, c=c, tk=se_kev):
                    y.wait_ge(ctx.sem("se"), tk)
                    ctx.inc(y.dma_start(out=kin.ap()[c * 128:(c + 1) * 128, :], in_=ktmp[:]), "kv", 16)

            wt_bv = load_ab("v", [("se", ctx.tok("se"))])
            for eh in range(2):
                ve_w = load_fold("v", slice(eh * 512, (eh + 1) * 512), slice(0, 4096))
                for t8 in range(16):
                    tslice = slice(t8 * 128, (t8 + 1) * 128)

                    @block.tensor
                    def _(t, t8=t8, eh=eh, ve_w=ve_w, tslice=tslice, wt_bv=wt_bv):
                        t.wait_ge(ctx.sem("ve"), ve_w)
                        t.wait_ge(ctx.sem("wt"), wt_bv)
                        t.wait_ge(ctx.sem("se"), ctx.tok("se"))
                        t.wait_ge(ctx.sem("kv"), ctx.tok("kv"))
                        for cc in range(NCH):
                            t.matmul(ppj[:, 0:512],
                                     znbuf[:, cc * T:(cc + 1) * T][:, tslice],
                                     wT[:, cc * 512:(cc + 1) * 512],
                                     start=(cc == 0), stop=False)
                        t.matmul(ppj[:, 0:512], onesrow_bf[:, 0:128],
                                 brow[:, eh * 512:(eh + 1) * 512], start=False, stop=True)
                        ctx.incd(t, "pe")
                    tk = ctx.tok("pe")

                    @block.scalar
                    def _(sx, eh=eh, tk=tk):
                        sx.wait_ge(ctx.sem("pe"), tk)
                        sx.copy(ktmp[:, eh * 512:(eh + 1) * 512], ppj[:, 0:512])
                        ctx.incd(sx, "se")
                    se_vev = ctx.tok("se")

                    @block.sync
                    def _(y, tslice=tslice, eh=eh, tk=se_vev):
                        y.wait_ge(ctx.sem("se"), tk)
                        ctx.inc(y.dma_start(out=vin.ap()[tslice, eh * 512:(eh + 1) * 512],
                                            in_=ktmp[:, eh * 512:(eh + 1) * 512]), "kv", 16)
            kv_all = ctx.tok("kv")

            if L1A_LIMIT < 3:
                kscope.close()
                return

            kscope.close()
            P4 = ss.enter_context(nc.sbuf_tensor("l1a_P4", [128, T], BF16))
            amul = scr2[:].bitcast(BF16)[:, 2048:2560]

            @block.gpsimd
            def _(g):
                g.wait_ge(ctx.sem("kv"), kv_all)
                ctx.inc(g.collective_compute("AllGather", ALU.bypass, replica_groups=PAIRS,
                                             ins=[kin.ap()], outs=[kout.ap()]), "cc", 1)
                ctx.inc(g.collective_compute("AllGather", ALU.bypass, replica_groups=PAIRS,
                                             ins=[vin.ap()], outs=[vout.ap()]), "cc", 1)
            cc_kv = ctx.tok("cc")

            @block.vector
            def _(v):
                v.memset(K65[64:65, :], 1.0)
                v.memset(vbuf[:].rearrange("p (kt c) -> p kt c", c=65)[:, :, 64:65], 1.0)
                ctx.incd(v, "ve")

            with ExitStack() as aps:
                psSX = aps.enter_context(nc.psum_tensor("l1a_psSX", [128, T], F32))
                psO = aps.enter_context(nc.psum_tensor("l1a_psO", [128, 512], F32))
                psB = aps.enter_context(nc.psum_tensor("l1a_psB", [64, 512], F32))

                wt_bq = load_ab("q", [("pe", ctx.tok("pe"))])

                for h in range(HEADS):
                    hp, hh = h // 2, h % 2
                    if hh == 0:
                        ve_w = load_fold("q", slice(hp * 128, (hp + 1) * 128), slice(0, 1024))
                        for tt in range(NTT):
                            @block.tensor
                            def _(t, hp=hp, tt=tt, ve_w=ve_w, wt_bq=wt_bq):
                                t.wait_ge(ctx.sem("ve"), ctx.tok("ve"))
                                t.wait_ge(ctx.sem("wt"), wt_bq)
                                t.wait_ge(ctx.sem("gl"), ctx.tok("gl"))
                                for cc in range(NCH):
                                    t.matmul(ppj[:, 0:512],
                                             wT[:, cc * 128:(cc + 1) * 128],
                                             znbuf[:, cc * T + tt * 512: cc * T + (tt + 1) * 512],
                                             start=(cc == 0), stop=False)
                                t.matmul(ppj[:, 0:512], brow[:, hp * 128:(hp + 1) * 128],
                                         onesrow_bf[:, 0:512], start=False, stop=True)
                                ctx.incd(t, "pe")
                            tk = ctx.tok("pe")

                            @block.vector
                            def _(v, tt=tt, tk=tk):
                                v.wait_ge(ctx.sem("pe"), tk)
                                v.tensor_scalar_mul(qpair[:, tt * 512:(tt + 1) * 512],
                                                    ppj[:, 0:512], 0.125)
                                ctx.incd(v, "ve")
                        ve_qp = ctx.tok("ve")
                    else:
                        ve_qp = ctx.tok("ve")

                    @block.sync
                    def _(y, h=h, cc_kv=cc_kv):
                        y.wait_ge(ctx.sem("cc"), cc_kv)
                        y.wait_ge(ctx.sem("se"), ctx.tok("se"))
                        y.wait_ge(ctx.sem("pe"), ctx.tok("pe"))
                        ctx.inc(y.dma_start(out=K65[0:64, 0:T], in_=kout.ap()[h * 64:(h + 1) * 64, :]), "kv", 16)
                        ctx.inc(y.dma_start(out=K65[0:64, T:2 * T], in_=kout.ap()[1024 + h * 64:1024 + (h + 1) * 64, :]), "kv", 16)
                        ctx.inc(y.dma_start(out=vbuf[:].rearrange("p (kt c) -> p kt c", c=65)[:, :, 0:64],
                                            in_=vout.ap().rearrange("(kt p) e -> p kt e", p=128)[:, :, h * 64:(h + 1) * 64]),
                                "kv", 16)
                    kv_head = ctx.tok("kv")

                    @block.gpsimd
                    def _(g, hh=hh, ve_qp=ve_qp):
                        g.wait_ge(ctx.sem("ve"), ve_qp)
                        g.wait_ge(ctx.sem("se"), ctx.tok("se"))
                        ctx.inc(g.dma_start(out=q65[0:64, :], in_=qpair[hh * 64:(hh + 1) * 64, :]), "gl", 16)
                    gl_q = ctx.tok("gl")

                    @block.scalar
                    def _(sx, tk=kv_head, tkg=gl_q):
                        sx.wait_ge(ctx.sem("kv"), tk)
                        sx.wait_ge(ctx.sem("gl"), tkg)
                        sx.wait_ge(ctx.sem("ve"), ctx.tok("ve"))
                        sx.activation(sq[0:64, :], q65[0:64, :], AF.Square)
                        sqk = scr1[:].bitcast(BF16)
                        sx.activation(sqk[0:64, 0:4096], K65[0:64, :], AF.Square)
                        ctx.incd(sx, "se")
                    se_sq = ctx.tok("se")

                    @block.vector
                    def _(v, tk=kv_head):
                        v.wait_ge(ctx.sem("kv"), tk)
                        v.reduce_max(scr2[0:64, 0:1], K65[0:64, :], axis=AX.X,
                                     apply_absolute_value=True)
                        v.drain()
                        v.tensor_mul(scr2[0:64, 1:2], scr2[0:64, 0:1], scr2[0:64, 0:1])
                        ctx.incd(v, "ve")
                    ve_km = ctx.tok("ve")

                    @block.tensor
                    def _(t, tk1=se_sq, tk2=ve_km):
                        t.wait_ge(ctx.sem("se"), tk1)
                        t.wait_ge(ctx.sem("ve"), tk2)
                        for i in range(NTT):
                            t.matmul(psSX[0:1, i * 512:(i + 1) * 512], onescol[0:64, :],
                                     sq[0:64, i * 512:(i + 1) * 512], start=True, stop=True)
                        t.matmul(psB[0:1, 0:1], onescol32[0:64, :], scr2[0:64, 1:2],
                                 start=True, stop=True)
                        ctx.incd(t, "pe")
                    pe_norm = ctx.tok("pe")

                    @block.scalar
                    def _(sx, tk=pe_norm):
                        sx.wait_ge(ctx.sem("pe"), tk)
                        for i in range(NTT):
                            sx.activation(rowbuf[0:1, i * 512:(i + 1) * 512],
                                          psSX[0:1, i * 512:(i + 1) * 512], AF.Sqrt)
                        sx.activation(rowbuf[0:1, T:T + 1], psB[0:1, 0:1], AF.Sqrt)
                        ctx.incd(sx, "se")
                    se_norm = ctx.tok("se")

                    @block.vector
                    def _(v, tk=se_norm):
                        v.wait_ge(ctx.sem("se"), tk)
                        v.tensor_scalar(rowbuf[0:1, 0:T], rowbuf[0:1, 0:T],
                                        rowbuf[0:1, T:T + 1], -1.0, ALU.mult, ALU.mult)
                        ctx.incd(v, "ve")
                    ve_cr = ctx.tok("ve")

                    @block.gpsimd
                    def _(g, tk=ve_cr):
                        g.wait_ge(ctx.sem("ve"), tk)
                        ctx.inc(g.dma_start(out=q65[64:65, :], in_=rowbuf[0:1, 0:T]), "gl", 16)
                    gl_cr = ctx.tok("gl")

                    for qt in range(NTT):
                        qsl = slice(qt * 512, (qt + 1) * 512)
                        for grp in range(8):
                            @block.tensor
                            def _(t, qt=qt, grp=grp, qsl=qsl, gl_cr=gl_cr):
                                if grp == 0 and qt == 0:
                                    t.wait_ge(ctx.sem("gl"), gl_cr)
                                    t.wait_ge(ctx.sem("ve"), ctx.tok("ve"))
                                t.wait_ge(ctx.sem("se"), ctx.tok("se"))
                                for b in range(4):
                                    kb = grp * 4 + b
                                    t.matmul(psSX[:, b * 512:(b + 1) * 512],
                                             K65[0:65, kb * 128:(kb + 1) * 128],
                                             q65[0:65, qsl], start=True, stop=True)
                                ctx.incd(t, "pe")
                            pe_s = ctx.tok("pe")

                            @block.scalar
                            def _(sx, tk=pe_s):
                                sx.wait_ge(ctx.sem("pe"), tk)
                                sx.wait_ge(ctx.sem("pe"), ctx.tok("pe"))
                                sx.activation(P4[:], psSX[:], AF.Exp)
                                ctx.incd(sx, "se")
                            se_exp = ctx.tok("se")

                            @block.tensor
                            def _(t, grp=grp, tk=se_exp):
                                t.wait_ge(ctx.sem("se"), tk)
                                for b in range(4):
                                    kb = grp * 4 + b
                                    t.matmul(psO[0:65, 0:512],
                                             vbuf[:, kb * 65:(kb + 1) * 65],
                                             P4[:, b * 512:(b + 1) * 512],
                                             start=(grp == 0 and b == 0),
                                             stop=(grp == 7 and b == 3))
                                ctx.incd(t, "pe")
                        pe_pv = ctx.tok("pe")

                        @block.vector
                        def _(v, tk=pe_pv):
                            v.wait_ge(ctx.sem("pe"), tk)
                            v.reciprocal(scr2[64:65, 0:512], psO[64:65, 0:512])
                            ctx.incd(v, "ve")
                        ve_rd = ctx.tok("ve")

                        @block.tensor
                        def _(t, tk=ve_rd):
                            t.wait_ge(ctx.sem("ve"), tk)
                            t.matmul(psB[0:64, 0:512], onesall[64:65, 0:64],
                                     scr2[64:65, 0:512], start=True, stop=True,
                                     tile_position=(64, 0))
                            ctx.incd(t, "pe")
                        pe_bc = ctx.tok("pe")

                        @block.vector
                        def _(v, tk=pe_bc):
                            v.wait_ge(ctx.sem("pe"), tk)
                            v.tensor_copy(scr2[0:64, 0:512], psB[0:64, 0:512])
                            v.drain()
                            v.tensor_mul(amul[0:64, 0:512], psO[0:64, 0:512], scr2[0:64, 0:512])
                            ctx.incd(v, "ve")
                        ve_am = ctx.tok("ve")

                        @block.sync
                        def _(y, h=h, qt=qt, tk=ve_am):
                            y.wait_ge(ctx.sem("ve"), tk)
                            hp2, hh2 = h // 2, h % 2
                            ctx.inc(y.dma_start(out=obuf[hh2 * 64:(hh2 + 1) * 64,
                                                         hp2 * T + qt * 512: hp2 * T + (qt + 1) * 512],
                                                in_=amul[0:64, 0:512]), "at", 16)

            if L1A_LIMIT < 4:
                return
            at_all = ctx.tok("at")
            wt_bo = load_ab("o", [("pe", ctx.tok("pe")), ("at", at_all)])
            for c in range(NCH):
                ve_w = load_fold("o", slice(c * 128, (c + 1) * 128), slice(0, 1024))
                for tt in range(NTT):
                    @block.tensor
                    def _(t, c=c, tt=tt, ve_w=ve_w, wt_bo=wt_bo, at_all=at_all):
                        t.wait_ge(ctx.sem("ve"), ve_w)
                        t.wait_ge(ctx.sem("wt"), wt_bo)
                        t.wait_ge(ctx.sem("at"), at_all)
                        t.wait_ge(ctx.sem("se"), ctx.tok("se"))
                        for cc in range(NCH):
                            t.matmul(ppj[:, 0:512],
                                     wT[:, cc * 128:(cc + 1) * 128],
                                     obuf[:, cc * T + tt * 512: cc * T + (tt + 1) * 512],
                                     start=(cc == 0), stop=False)
                        t.matmul(ppj[:, 0:512], brow[:, c * 128:(c + 1) * 128],
                                 onesrow_bf[:, 0:512], start=False, stop=True)
                        ctx.incd(t, "pe")
                    tk = ctx.tok("pe")

                    @block.scalar
                    def _(sx, c=c, tt=tt, tk=tk):
                        sx.wait_ge(ctx.sem("pe"), tk)
                        sx.copy(znbuf[:, c * T + tt * 512: c * T + (tt + 1) * 512], ppj[:, 0:512])
                        ctx.incd(sx, "se")

    emit_postln(env, znbuf)


def _l1a_inputs(inputs):
    d = {}
    for n in "qkvo":
        d[f"W{n}T"] = np.ascontiguousarray(np.asarray(inputs[f"L1_attn_W{n}"], np.float32).T)
        A = np.asarray(inputs[f"L1_attn_A{n}"], np.float32)
        Bm = np.asarray(inputs[f"L1_attn_B{n}"], np.float32)
        a32 = np.zeros((32, D), np.float32); a32[0:8] = A
        b32 = np.zeros((32, D), np.float32); b32[0:8] = Bm.T
        d[f"A{n}32"] = a32
        d[f"B{n}T32"] = b32
        d[f"b{n}"] = np.asarray(inputs[f"L1_attn_b{n}"], np.float32).reshape(1, D).copy()
    return d



def emit_l0a(env):
    nc, block, ctx, es, dr = env["nc"], env["block"], env["ctx"], env["es"], env["dr"]
    x_res, znbuf, obuf = env["x_res"], env["znbuf"], env["obuf"]
    onesrow, onescol, ident = env["onesrow"], env["onescol"], env["ident"]
    scr1, scr2, sq, glamask, rolecol = env["scr1"], env["scr2"], env["sq"], env["glamask"], env["rolecol"]

    for n in "qkvgo":
        dr[f"gW{n}T"] = nc.dram_tensor(f"gW{n}T", [D, D], F32, kind="ExternalInput").ap()
    dr["gamcol"] = nc.dram_tensor("gamcol", [128, 8], F32, kind="ExternalInput").ap()
    dr["betcol"] = nc.dram_tensor("betcol", [128, 8], F32, kind="ExternalInput").ap()

    gla_in = nc.dram_tensor("l0a_gin", [128, 512], F32, kind="Internal")
    gla_out = nc.dram_tensor("l0a_gout", [256, 512], F32, kind="Internal")

    emit_preln(env)
    if L0A_LIMIT < 2:
        return

    with ExitStack() as ss:
        wpair = ss.enter_context(nc.sbuf_tensor("l0a_wp", [128, 8 * 128], BF16))
        q0 = ss.enter_context(nc.sbuf_tensor("l0a_q0", [128, T], BF16))
        k0 = ss.enter_context(nc.sbuf_tensor("l0a_k0", [128, T], BF16))
        pkh = ss.enter_context(nc.sbuf_tensor("l0a_pkh", [128, T], BF16))
        v0c = ss.enter_context(nc.sbuf_tensor("l0a_v0", [64, 32 * 128], BF16))
        Ebuf = ss.enter_context(nc.sbuf_tensor("l0a_E", [128, T], BF16))
        sm = ss.enter_context(nc.sbuf_tensor("l0a_sm", [128, 128], F32))
        zcol = ss.enter_context(nc.sbuf_tensor("l0a_z", [128, 1], F32))
        ktok = ss.enter_context(nc.sbuf_tensor("l0a_kt", [128, 128], BF16))
        Pm = ss.enter_context(nc.sbuf_tensor("l0a_Pm", [128, 128], BF16))
        H = ss.enter_context(nc.sbuf_tensor("l0a_H", [128, 64], BF16))
        Hfin = ss.enter_context(nc.sbuf_tensor("l0a_Hf", [128, 512], F32))
        Huse = ss.enter_context(nc.sbuf_tensor("l0a_Hu", [128, 512], F32))
        gcol = ss.enter_context(nc.sbuf_tensor("l0a_gc", [128, 8], F32))
        bcol = ss.enter_context(nc.sbuf_tensor("l0a_bc", [128, 8], F32))
        cum = scr2
        Sc = scr1

        @block.sync
        def _(y):
            ctx.inc(y.dma_start(out=gcol[:], in_=dr["gamcol"]), "ld", 16)
            ctx.inc(y.dma_start(out=bcol[:], in_=dr["betcol"]), "ld", 16)
        ld_gb = ctx.tok("ld")

        @block.vector
        def _(v):
            v.memset(zcol[:], 0.0)
            ctx.incd(v, "ve")

        with ExitStack() as ps:
            ppj = ps.enter_context(nc.psum_tensor("l0a_ppj", [128, 512], F32))
            psHA = ps.enter_context(nc.psum_tensor("l0a_psHA", [128, 128], F32))
            psT = ps.enter_context(nc.psum_tensor("l0a_psT", [128, 128], BF16))
            psS2 = ps.enter_context(nc.psum_tensor("l0a_psS2", [128, 64], F32))
            psS2b = ps.enter_context(nc.psum_tensor("l0a_psS2b", [128, 64], F32))
            psZ = ps.enter_context(nc.psum_tensor("l0a_psZ", [128, 64], F32))
            psO = ps.enter_context(nc.psum_tensor("l0a_psO", [128, 64], F32))
            psO2 = ps.enter_context(nc.psum_tensor("l0a_psO2", [128, 64], F32))
            otmp = ss.enter_context(nc.sbuf_tensor("l0a_otmp", [128, 64], F32))

            def proj_feat(wname, hp, dst, evac_scalar=True):
                @block.gpsimd
                def _(g):
                    g.wait_ge(ctx.sem("pe"), ctx.tok("pe"))
                    ctx.inc(g.dma_start(
                        out=wpair[:].rearrange("p (cc e) -> p cc e", e=128),
                        in_=dr[wname][:, hp * 128:(hp + 1) * 128].rearrange("(cc p) e -> p cc e", p=128)),
                        "wt", 16)
                wtk = ctx.tok("wt")

                for tt in range(NTT):
                    @block.tensor
                    def _(t, tt=tt, wtk=wtk):
                        t.wait_ge(ctx.sem("wt"), wtk)
                        t.wait_ge(ctx.sem("se"), ctx.tok("se"))
                        t.wait_ge(ctx.sem("ve"), ctx.tok("ve"))
                        for cc in range(NCH):
                            t.matmul(ppj[:, 0:512],
                                     wpair[:, cc * 128:(cc + 1) * 128],
                                     znbuf[:, cc * T + tt * 512: cc * T + (tt + 1) * 512],
                                     start=(cc == 0), stop=(cc == NCH - 1))
                        ctx.incd(t, "pe")
                    tk = ctx.tok("pe")

                    if evac_scalar:
                        @block.scalar
                        def _(sx, tt=tt, tk=tk, dst=dst):
                            sx.wait_ge(ctx.sem("pe"), tk)
                            sx.copy(dst[:, tt * 512:(tt + 1) * 512], ppj[:, 0:512])
                            ctx.incd(sx, "se")
                    else:
                        @block.scalar
                        def _(sx, tt=tt, tk=tk):
                            sx.wait_ge(ctx.sem("pe"), tk)
                            sx.activation(Ebuf[:, tt * 512:(tt + 1) * 512], ppj[:, 0:512],
                                          AF.Exp, scale=-1.0)
                            sx.activation(cum[:, tt * 512:(tt + 1) * 512],
                                          Ebuf[:, tt * 512:(tt + 1) * 512], AF.Ln, bias=1.0)
                            ctx.incd(sx, "se")

            def proj_vtok(hp):
                @block.gpsimd
                def _(g):
                    g.wait_ge(ctx.sem("pe"), ctx.tok("pe"))
                    ctx.inc(g.dma_start(
                        out=wpair[:].rearrange("p (cc e) -> p cc e", e=128),
                        in_=dr["gWvT"][:, hp * 128:(hp + 1) * 128].rearrange("(cc p) e -> p cc e", p=128)),
                        "wt", 16)
                wtk = ctx.tok("wt")

                for j in range(32):
                    @block.tensor
                    def _(t, j=j, wtk=wtk):
                        t.wait_ge(ctx.sem("wt"), wtk)
                        t.wait_ge(ctx.sem("se"), ctx.tok("se"))
                        for cc in range(NCH):
                            t.matmul(ppj[0:64, 0:128],
                                     znbuf[:, cc * T + j * 64: cc * T + (j + 1) * 64],
                                     wpair[:, cc * 128:(cc + 1) * 128],
                                     start=(cc == 0), stop=(cc == NCH - 1))
                        ctx.incd(t, "pe")
                    tk = ctx.tok("pe")

                    @block.scalar
                    def _(sx, j=j, tk=tk):
                        sx.wait_ge(ctx.sem("pe"), tk)
                        sx.copy(v0c[0:64, j * 128:(j + 1) * 128], ppj[0:64, 0:128])
                        ctx.incd(sx, "se")

            def emit_scan():
                @block.vector
                def _(v):
                    v.wait_ge(ctx.sem("se"), ctx.tok("se"))
                    v.tensor_tensor_scan(Sc[:], cum[:], zcol[:].broadcast_to([128, T]),
                                         0.0, ALU.add, ALU.add)
                    v.drain()
                    v.tensor_copy(cum[:], Sc[:])
                    v.drain()
                    v.tensor_copy(sm[:, 32:64].rearrange("p (c o) -> p c o", o=1),
                                  cum[:].rearrange("p (c f) -> p c f", f=64)[:, :, 63:64])
                    v.drain()
                    v.memset(sm[:, 0:1], 0.0)
                    v.tensor_copy(sm[:, 1:32], sm[:, 32:63])
                    v.drain()
                    v.tensor_tensor(sm[:, 64:96], sm[:, 32:64], sm[:, 0:32], ALU.subtract)
                    ctx.incd(v, "ve")

            for hp in range(8):
                proj_feat("gWkT", hp, k0)
                proj_feat("gWgT", hp, None, evac_scalar=False)
                proj_vtok(hp)
                emit_scan()
                ve_scan = ctx.tok("ve")

                @block.vector
                def _(v, tk=ve_scan):
                    v.wait_ge(ctx.sem("ve"), tk)
                    v.tensor_scalar(Sc[:], cum[:], cum[:, T - 1:T], None, ALU.subtract)
                    ctx.incd(v, "ve")
                ve_arg = ctx.tok("ve")

                @block.scalar
                def _(sx, tk=ve_arg):
                    sx.wait_ge(ctx.sem("ve"), tk)
                    sx.activation(Ebuf[:], Sc[:], AF.Exp)
                    ctx.incd(sx, "se")
                se_e = ctx.tok("se")

                @block.vector
                def _(v, tk=se_e):
                    v.wait_ge(ctx.sem("se"), tk)
                    v.tensor_mul(k0[:], k0[:], Ebuf[:])
                    ctx.incd(v, "ve")
                ve_kb = ctx.tok("ve")

                if L0A_LIMIT < 3:
                    continue
                for j in range(32):
                    @block.tensor
                    def _(t, j=j, tk=ve_kb):
                        t.wait_ge(ctx.sem("ve"), ctx.tok("ve"))
                        if j == 0:
                            t.wait_ge(ctx.sem("ve"), tk)
                        t.transpose(psT[0:64, 0:128], k0[:, j * 64:(j + 1) * 64], ident[:])
                        ctx.incd(t, "pe")
                    tk2 = ctx.tok("pe")

                    @block.vector
                    def _(v, j=j, tk=tk2):
                        v.wait_ge(ctx.sem("pe"), tk)
                        v.tensor_copy(ktok[0:64, :], psT[0:64, 0:128])
                        ctx.incd(v, "ve")
                    ve_kt = ctx.tok("ve")

                    @block.tensor
                    def _(t, j=j, tk=ve_kt, hp=hp):
                        t.wait_ge(ctx.sem("ve"), tk)
                        t.matmul(psHA[:], ktok[0:64, :], v0c[0:64, j * 128:(j + 1) * 128],
                                 start=(j == 0), stop=(j == 31))
                        ctx.incd(t, "pe")
                pe_hf = ctx.tok("pe")

                @block.vector
                def _(v, hp=hp, tk=pe_hf):
                    v.wait_ge(ctx.sem("pe"), tk)
                    v.tensor_copy(Hfin[0:64, hp * 64:(hp + 1) * 64], psHA[0:64, 0:64])
                    v.tensor_copy(Hfin[64:128, hp * 64:(hp + 1) * 64], psHA[64:128, 64:128])
                    ctx.incd(v, "ve")

            if L0A_LIMIT < 4:
                return
            ve_hf = ctx.tok("ve")

            @block.gpsimd
            def _(g):
                g.wait_ge(ctx.sem("ve"), ve_hf)
                ctx.inc(g.dma_start(out=gla_in.ap(), in_=Hfin[:]), "gl", 16)
                g.wait_ge(ctx.sem("gl"), ctx.tok("gl"))
                ctx.inc(g.collective_compute("AllGather", ALU.bypass, replica_groups=PAIRS,
                                             ins=[gla_in.ap()], outs=[gla_out.ap()]), "cc", 1)
                g.wait_ge(ctx.sem("cc"), ctx.tok("cc"))
                ctx.inc(g.dma_start(out=Huse[:], in_=gla_out.ap()[0:128, :]), "gl", 16)
            gl_h = ctx.tok("gl")

            @block.vector
            def _(v, tk=gl_h):
                v.wait_ge(ctx.sem("gl"), tk)
                v.tensor_scalar(Huse[:], Huse[:], rolecol[:, 0:1], None, ALU.mult)
                ctx.incd(v, "ve")

            if L0A_LIMIT < 5:
                return
            for hp in range(8):
                proj_feat("gWqT", hp, q0)
                proj_feat("gWkT", hp, k0)
                proj_feat("gWgT", hp, None, evac_scalar=False)
                proj_vtok(hp)
                emit_scan()
                ve_scan = ctx.tok("ve")

                @block.scalar
                def _(sx, tk=ve_scan):
                    sx.wait_ge(ctx.sem("ve"), tk)
                    sx.activation(sm[:, 96:128], sm[:, 64:96], AF.Exp, scale=-1.0)
                    ctx.incd(sx, "se")

                @block.vector
                def _(v, tk=ve_scan):
                    v.wait_ge(ctx.sem("ve"), tk)
                    cv = cum[:].rearrange("p (c f) -> p c f", f=64)
                    v.tensor_tensor(Sc[:].rearrange("p (c f) -> p c f", f=64), cv,
                                    sm[:, 0:32].rearrange("p (c o) -> p c o", o=1).broadcast_to([128, 32, 64]),
                                    ALU.subtract)
                    v.drain()
                    v.tensor_tensor(cv, cv,
                                    sm[:, 32:64].rearrange("p (c o) -> p c o", o=1).broadcast_to([128, 32, 64]),
                                    ALU.subtract)
                    ctx.incd(v, "ve")
                ve_sc = ctx.tok("ve")

                @block.scalar
                def _(sx, tk=ve_sc):
                    sx.wait_ge(ctx.sem("ve"), tk)
                    sx.activation(Ebuf[:], cum[:], AF.Exp)
                    ctx.incd(sx, "se")
                se_e2 = ctx.tok("se")

                @block.vector
                def _(v, tk=se_e2):
                    v.wait_ge(ctx.sem("se"), tk)
                    v.tensor_mul(pkh[:], k0[:], Ebuf[:])
                    ctx.incd(v, "ve")
                ve_kh = ctx.tok("ve")

                @block.scalar
                def _(sx, tk=ve_kh):
                    sx.wait_ge(ctx.sem("ve"), tk)
                    sx.activation(Ebuf[:], Sc[:], AF.Exp, scale=-1.0)
                    ctx.incd(sx, "se")
                se_ep = ctx.tok("se")

                @block.vector
                def _(v, tk=se_ep):
                    v.wait_ge(ctx.sem("se"), tk)
                    v.tensor_mul(q0[:], q0[:], Ebuf[:])
                    v.drain()
                    v.tensor_scalar_min(Sc[:], Sc[:], 75.0)
                    ctx.incd(v, "ve")
                ve_qt = ctx.tok("ve")

                @block.scalar
                def _(sx, tk=ve_qt):
                    sx.wait_ge(ctx.sem("ve"), tk)
                    sx.activation(Ebuf[:], Sc[:], AF.Exp)
                    ctx.incd(sx, "se")
                se_em = ctx.tok("se")

                @block.vector
                def _(v, tk=se_em, hp=hp):
                    v.wait_ge(ctx.sem("se"), tk)
                    v.tensor_mul(k0[:], k0[:], Ebuf[:])
                    v.drain()
                    v.tensor_copy(H[:], Huse[:, hp * 64:(hp + 1) * 64])
                    ctx.incd(v, "ve")
                ve_kt2 = ctx.tok("ve")

                if L0A_LIMIT < 6:
                    continue
                for j in range(32):
                    jsl = slice(j * 64, (j + 1) * 64)

                    @block.tensor
                    def _(t, j=j, jsl=jsl, tk=ve_kt2):
                        if j == 0:
                            t.wait_ge(ctx.sem("ve"), tk)
                        else:
                            t.wait_ge(ctx.sem("ve"), ctx.tok("ve"))
                        if L0A_CHUNK >= 1:
                            t.matmul(psS2[0:64, 0:64], k0[0:64, jsl], q0[0:64, jsl],
                                     start=True, stop=True)
                            t.matmul(psS2b[0:64, 0:64], k0[64:128, jsl], q0[64:128, jsl],
                                     start=True, stop=True, tile_position=(64, 0))
                        if L0A_CHUNK >= 2:
                            t.transpose(psT[0:64, 0:128], pkh[:, jsl], ident[:])
                        ctx.incd(t, "pe")
                    pe_s = ctx.tok("pe")

                    @block.vector
                    def _(v, tk=pe_s):
                        v.wait_ge(ctx.sem("pe"), tk)
                        if L0A_CHUNK >= 1:
                            v.tensor_mul(Pm[0:64, 0:64], psS2[0:64, 0:64], glamask[0:64, 0:64])
                            v.tensor_mul(Pm[0:64, 64:128], psS2b[0:64, 0:64], glamask[0:64, 64:128])
                            v.drain()
                        if L0A_CHUNK >= 2:
                            v.tensor_copy(ktok[0:64, :], psT[0:64, 0:128])
                        ctx.incd(v, "ve")
                    ve_pm = ctx.tok("ve")

                    @block.tensor
                    def _(t, j=j, jsl=jsl, tk=ve_pm):
                        t.wait_ge(ctx.sem("ve"), tk)
                        if L0A_CHUNK >= 3:
                            t.matmul(psZ[0:64, 0:64], ktok[0:64, 0:64],
                                     v0c[0:64, j * 128: j * 128 + 64],
                                     start=True, stop=True)
                            t.matmul(psZ[64:128, 0:64], ktok[0:64, 64:128],
                                     v0c[0:64, j * 128 + 64: (j + 1) * 128],
                                     start=True, stop=True, tile_position=(0, 64))
                        if L0A_CHUNK >= 4:
                            t.matmul(psO[0:64, 0:64], v0c[0:64, j * 128: j * 128 + 64],
                                     Pm[0:64, 0:64], start=True, stop=True)
                            t.matmul(psO[64:128, 0:64], v0c[0:64, j * 128 + 64: (j + 1) * 128],
                                     Pm[0:64, 64:128], start=True, stop=True,
                                     tile_position=(0, 64))
                        if L0A_CHUNK >= 5:
                            t.matmul(psO2[0:64, 0:64], H[0:64, 0:64], q0[0:64, jsl],
                                     start=True, stop=True)
                            t.matmul(psO2[64:128, 0:64], H[64:128, 0:64], q0[64:128, jsl],
                                     start=True, stop=True, tile_position=(64, 64))
                        ctx.incd(t, "pe")
                    pe_o = ctx.tok("pe")

                    @block.vector
                    def _(v, j=j, hp=hp, jsl=jsl, tk=pe_o):
                        v.wait_ge(ctx.sem("pe"), tk)
                        if L0A_CHUNK >= 6:
                            v.scalar_tensor_tensor(H[:], H[:], sm[:, 96 + j:97 + j], psZ[:, 0:64],
                                                   ALU.mult, ALU.add)
                            v.drain()
                            v.tensor_copy(otmp[:], psO2[:, 0:64])
                            v.drain()
                            v.tensor_tensor(obuf[:, hp * T + j * 64: hp * T + (j + 1) * 64],
                                            psO[:, 0:64], otmp[:], ALU.add)
                        ctx.incd(v, "ve")

            @block.scalar
            def _(sx):
                sx.wait_ge(ctx.sem("ld"), ld_gb)
        if L0A_LIMIT < 7:
            return
        ln_tok = emit_ln(env, obuf, "gb", zn=znbuf, gamma=gcol, beta=bcol)
        if L0A_LIMIT < 8:
            return

        with ExitStack() as ps2:
            ppj2 = ps2.enter_context(nc.psum_tensor("l0a_ppj2", [128, 512], F32))
            for c in range(NCH):
                @block.gpsimd
                def _(g, c=c):
                    g.wait_ge(ctx.sem("pe"), ctx.tok("pe"))
                    ctx.inc(g.dma_start(
                        out=wpair[:].rearrange("p (cc e) -> p cc e", e=128),
                        in_=dr["gWoT"][:, c * 128:(c + 1) * 128].rearrange("(cc p) e -> p cc e", p=128)),
                        "wt", 16)
                wtk = ctx.tok("wt")

                for tt in range(NTT):
                    @block.tensor
                    def _(t, c=c, tt=tt, wtk=wtk):
                        t.wait_ge(ctx.sem("wt"), wtk)
                        t.wait_ge(ctx.sem("se"), ctx.tok("se"))
                        if c == 0 and tt == 0:
                            t.wait_ge(ctx.sem("ve"), ctx.tok("ve"))
                        for cc in range(NCH):
                            t.matmul(ppj2[:, 0:512],
                                     wpair[:, cc * 128:(cc + 1) * 128],
                                     znbuf[:, cc * T + tt * 512: cc * T + (tt + 1) * 512],
                                     start=(cc == 0), stop=(cc == NCH - 1))
                        ctx.incd(t, "pe")
                    tk = ctx.tok("pe")

                    @block.scalar
                    def _(sx, c=c, tt=tt, tk=tk):
                        sx.wait_ge(ctx.sem("pe"), tk)
                        sx.copy(obuf[:, c * T + tt * 512: c * T + (tt + 1) * 512], ppj2[:, 0:512])
                        ctx.incd(sx, "se")

    emit_postln(env, obuf)


def _l0a_inputs(inputs):
    d = {}
    for n in "qkvgo":
        d[f"gW{n}T"] = np.ascontiguousarray(np.asarray(inputs[f"L0_attn_W{n}"], np.float32).T)
    d["gamcol"] = np.asarray(inputs["L0_attn_gamma"], np.float32).reshape(8, 128).T.copy()
    d["betcol"] = np.asarray(inputs["L0_attn_beta"], np.float32).reshape(8, 128).T.copy()
    return d


EMITTERS = {"l1f": emit_l1f, "l0f": emit_l0f, "l1a": emit_l1a, "l0a": emit_l0a}




def build_program(blocks):
    nc = bass.Bass("TRN2", target_bir_lowering=False, debug=False,
                   detect_race_conditions=False)
    dr = {}
    dr["x_in"] = nc.dram_tensor("x_in", [D, T], F32, kind="ExternalInput").ap()
    dr["x_out"] = nc.dram_tensor("x_out", [D, T], F32, kind="ExternalOutput").ap()
    for nm, sh, dt in [("ident", [128, 128], BF16), ("onescol", [128, 1], BF16),
                       ("onesrow", [1, 128], F32), ("glamask", [128, 128], F32),
                       ("rolecol", [128, 1], F32), ("epscol", [128, 1], F32),
                       ("ident32", [128, 16], F32), ("onesall", [128, 64], F32),
                       ("onescol32", [128, 1], F32), ("onesrow_bf", [1, 512], BF16)]:
        dr[nm] = nc.dram_tensor(nm, sh, dt, kind="ExternalInput").ap()

    es = ExitStack()
    block = es.enter_context(nc.Block())
    ctx = Ctx(nc, block)

    x_res = es.enter_context(nc.sbuf_tensor("x_res", [128, NCH * T], F32))
    znbuf = es.enter_context(nc.sbuf_tensor("znbuf", [128, NCH * T], BF16))
    obuf = es.enter_context(nc.sbuf_tensor("obuf", [128, NCH * T], BF16))
    rowbuf = es.enter_context(nc.sbuf_tensor("rowbuf", [128, 2 * T], F32))
    sq = es.enter_context(nc.sbuf_tensor("sq", [128, T], BF16))
    scr1 = es.enter_context(nc.sbuf_tensor("scr1", [128, T], F32))
    scr2 = es.enter_context(nc.sbuf_tensor("scr2", [128, T], F32))
    ident = es.enter_context(nc.sbuf_tensor("identt", [128, 128], BF16))
    onescol = es.enter_context(nc.sbuf_tensor("onescolt", [128, 1], BF16))
    onesrow = es.enter_context(nc.sbuf_tensor("onesrowt", [1, 128], F32))
    glamask = es.enter_context(nc.sbuf_tensor("glamaskt", [128, 128], F32))
    rolecol = es.enter_context(nc.sbuf_tensor("rolecolt", [128, 1], F32))
    epscol = es.enter_context(nc.sbuf_tensor("epscolt", [128, 1], F32))
    ident32 = es.enter_context(nc.sbuf_tensor("ident32t", [128, 16], F32))
    onesall = es.enter_context(nc.sbuf_tensor("onesallt", [128, 64], F32))
    onescol32 = es.enter_context(nc.sbuf_tensor("onescol32t", [128, 1], F32))
    onesrow_bf = es.enter_context(nc.sbuf_tensor("onesrowbft", [1, 512], BF16))

    @block.sync
    def _(y):
        for c in range(NCH):
            ctx.inc(y.dma_start(out=chunk(x_res, c), in_=dr["x_in"][c * 128:(c + 1) * 128, :]),
                    "ld", 16)
        for nm, tile in [("ident", ident), ("onescol", onescol), ("onesrow", onesrow),
                         ("glamask", glamask), ("rolecol", rolecol), ("epscol", epscol),
                         ("ident32", ident32), ("onesall", onesall), ("onescol32", onescol32),
                         ("onesrow_bf", onesrow_bf)]:
            ctx.inc(y.dma_start(out=tile[:], in_=dr[nm]), "ld", 16)
    ld_x = ctx.tok("ld")

    for eng in ("tensor", "vector", "scalar", "gpsimd"):
        def gate(e):
            e.wait_ge(ctx.sem("ld"), ld_x)
        getattr(block, eng)(gate)

    env = dict(nc=nc, block=block, ctx=ctx, es=es, dr=dr, x_res=x_res, znbuf=znbuf,
               obuf=obuf, rowbuf=rowbuf, sq=sq, scr1=scr1, scr2=scr2, ident=ident,
               onescol=onescol, onesrow=onesrow, glamask=glamask, rolecol=rolecol,
               epscol=epscol, ident32=ident32, onesall=onesall, onescol32=onescol32,
               onesrow_bf=onesrow_bf)

    for b in blocks:
        EMITTERS[b](env)

    @block.sync
    def _(y):
        ctx.wait_compute(y)
        for c in range(NCH):
            ctx.inc(y.dma_start(out=dr["x_out"][c * 128:(c + 1) * 128, :],
                                in_=chunk(x_res, c)), "st", 16)
        y.wait_ge(ctx.sem("st"), ctx.tok("st"))

    es.close()
    return nc


def _consts(core):
    hf = core % 2
    tri = np.triu(np.ones((64, 64), np.float32))
    return {
        "ident": np.eye(128).astype(BF),
        "onescol": np.ones((128, 1), BF),
        "onesrow": np.ones((1, 128), np.float32),
        "glamask": np.tile(tri, (2, 2)).astype(np.float32),
        "rolecol": np.full((128, 1), float(hf), np.float32),
        "epscol": np.full((128, 1), 1e-5, np.float32),
        "ident32": np.eye(128)[:, 0:16].astype(np.float32),
        "onesall": np.ones((128, 64), np.float32),
        "onescol32": np.ones((128, 1), np.float32),
        "onesrow_bf": np.ones((1, 512), BF),
    }


def _shard_x(x_full):
    out = []
    for c in range(8):
        b, hf = c // 2, c % 2
        out.append(np.ascontiguousarray(x_full[b, hf * T:(hf + 1) * T, :].T))
    return out


def _unshard_x(shards):
    x = np.empty((4, 4096, D), np.float32)
    for c in range(8):
        b, hf = c // 2, c % 2
        x[b, hf * T:(hf + 1) * T, :] = shards[c].T
    return x


def _l1f_inputs(inputs):
    d = {}
    for i in (1, 2, 3, 4):
        d[f"cn_w{i}"] = np.asarray(inputs[f"L1_ffn_cn_w{i}"], np.float32)
    for i in (1, 2, 3):
        d[f"gc_w{i}"] = np.asarray(inputs[f"L1_ffn_gc_w{i}"], np.float32)
    d["cn_b1"] = np.asarray(inputs["L1_ffn_cn_b1"], np.float32).reshape(8, 128).T.copy()
    d["cn_b2"] = np.asarray(inputs["L1_ffn_cn_b2"], np.float32).reshape(4, 128).T.copy()
    d["cn_b3"] = np.asarray(inputs["L1_ffn_cn_b3"], np.float32).reshape(2, 128).T.copy()
    d["cn_b4"] = np.asarray(inputs["L1_ffn_cn_b4"], np.float32).reshape(1, 1).copy()
    d["gc_b1"] = np.asarray(inputs["L1_ffn_gc_b1"], np.float32).reshape(4, 128).T.copy()
    d["gc_b2"] = np.asarray(inputs["L1_ffn_gc_b2"], np.float32).reshape(2, 128).T.copy()
    d["gc_b3"] = np.asarray(inputs["L1_ffn_gc_b3"], np.float32).reshape(1, 1).copy()
    d["arange"] = np.asarray(inputs["L1_ffn_adaptive_range"], np.float32).reshape(1, 1).copy()
    return d


BLOCK_INPUTS = {"l1f": _l1f_inputs, "l0f": _l0f_inputs, "l1a": _l1a_inputs, "l0a": _l0a_inputs}


def run_blocks(inputs, x_shards, blocks):
    per_block = {}
    for b in blocks:
        per_block.update(BLOCK_INPUTS[b](inputs))
    nc = build_program(blocks)
    in_maps = []
    for c in range(8):
        m = dict(_consts(c))
        m["x_in"] = np.ascontiguousarray(x_shards[c])
        m.update(per_block)
        in_maps.append(m)
    if os.environ.get("K_SIM"):
        from concourse import bass_interp
        sim = bass_interp.MultiCoreSim(nc, 2)
        for c in range(2):
            for k, v in in_maps[c].items():
                sim.cores[c].tensor(k)[:] = v
        sim.simulate()
        outs = [np.asarray(sim.cores[c].mem_tensor("x_out")) for c in range(2)]
        return outs + [np.zeros((D, T), np.float32)] * 6
    res = run_bass_kernel_spmd(nc, in_maps, core_ids=list(range(8)))
    return [res.results[c]["x_out"] for c in range(8)]


def kernel(**inputs):
    x = np.asarray(inputs["L0_attn_x"], np.float32)
    shards = _shard_x(x)
    for blocks in (["l0a"], ["l0f"], ["l1a"], ["l1f"]):
        if blocks[0] in EMITTERS:
            shards = run_blocks(inputs, shards, blocks)
    return _unshard_x(shards)
